# version 01
# baseline (speedup 1.0000x reference)
import numpy as np
import ml_dtypes
import concourse.bass as bass
import concourse.mybir as mybir
from concourse.bass_utils import run_bass_kernel_spmd
from contextlib import ExitStack

F32 = mybir.dt.float32
BF16 = mybir.dt.bfloat16
AF = mybir.ActivationFunctionType
ALU = mybir.AluOpType
AX = mybir.AxisListType

T = 2048
DM = 1024
DI = 2048
FF = 4096
S = 16
DTR = 64
KD = DM // 128
NI = DI // 128
NFH = 16
NE = 10
TQ = 512
EPS = 1e-5
NEG_BIG = -1e30


class Sem:
    def __init__(self, ctx, nc, name):
        self.h = ctx.enter_context(nc.semaphore(name))
        self.n = 0

    def inc(self, instr, k=1):
        instr.then_inc(self.h, k)
        self.n += k
        return self.n


def build_nc(debug=False, phases=7):
    nc = bass.Bass("TRN2", target_bir_lowering=False)

    x_h = nc.dram_tensor("x", [T, DM], F32, kind="ExternalInput")
    w_in = nc.dram_tensor("w_in", [KD, 128, 2 * DI], F32, kind="ExternalInput")
    normw = nc.dram_tensor("normw", [KD, 128, 1], F32, kind="ExternalInput")
    wcv = nc.dram_tensor("wcv", [NI, 128, 4], F32, kind="ExternalInput")
    bcv = nc.dram_tensor("bcv", [NI, 128, 1], F32, kind="ExternalInput")
    wxp = nc.dram_tensor("wxp", [NI, 128, 96], F32, kind="ExternalInput")
    wdt = nc.dram_tensor("wdt", [DTR, DI], F32, kind="ExternalInput")
    bdt = nc.dram_tensor("bdt", [NI, 128, 1], F32, kind="ExternalInput")
    alog = nc.dram_tensor("alog", [NI, 128, S], F32, kind="ExternalInput")
    dvec = nc.dram_tensor("dvec", [NI, 128, 1], F32, kind="ExternalInput")
    wout = nc.dram_tensor("wout", [NI, 128, DM], F32, kind="ExternalInput")
    centT = nc.dram_tensor("centT", [KD, 128, 8], F32, kind="ExternalInput")
    biasr = nc.dram_tensor("biasr", [1, 8], F32, kind="ExternalInput")
    wfcT = nc.dram_tensor("wfcT", [NE, DM, FF], F32, kind="ExternalInput")
    wprT = nc.dram_tensor("wprT", [NE, FF, DM], F32, kind="ExternalInput")
    ident = nc.dram_tensor("ident", [128, 128], BF16, kind="ExternalInput")
    out_h = nc.dram_tensor("out", [T, DM], F32, kind="ExternalOutput")

    skind = "ExternalOutput" if debug else "Internal"
    u_s = nc.dram_tensor("u_s", [NI, 128, T], F32, kind=skind)
    res_s = nc.dram_tensor("res_s", [NI, 128, T], F32, kind=skind)
    y_s = nc.dram_tensor("y_s", [NI, 128, T], BF16, kind=skind)
    x2_s = nc.dram_tensor("x2_s", [T, DM], F32, kind=skind)
    bc_s = nc.dram_tensor("bc_s", [32, T], F32, kind=skind)
    ub_s = nc.dram_tensor("ub_s", [NI, 128, T], BF16, kind=skind)
    if debug:
        dbg_dt = nc.dram_tensor("dbg_dt", [DTR, T], F32, kind="ExternalOutput")
        dbg_cmb = nc.dram_tensor("dbg_cmb", [128, 128], F32, kind="ExternalOutput")

    root = ExitStack()
    with root:
        sQ = Sem(root, nc, "sQ")
        sW = Sem(root, nc, "sW")
        sG = Sem(root, nc, "sG")
        sT = Sem(root, nc, "sT")
        sV = Sem(root, nc, "sV")
        sS = Sem(root, nc, "sS")
        sP = Sem(root, nc, "sP")

        def snap():
            return {sQ.h: sQ.n, sW.h: sW.n, sG.h: sG.n, sT.h: sT.n,
                    sV.h: sV.n, sS.h: sS.n, sP.h: sP.n}

        B5 = {}

        def bar(eng, sn):
            for h, v in sn.items():
                if v:
                    eng.wait_ge(h, v)

        A_neg = root.enter_context(nc.sbuf_tensor("A_neg", [128, NI * S], F32))
        bdt_pp = root.enter_context(nc.sbuf_tensor("bdt_pp", [128, NI], F32))
        bcv_pp = root.enter_context(nc.sbuf_tensor("bcv_pp", [128, NI], F32))
        d_pp = root.enter_context(nc.sbuf_tensor("d_pp", [128, NI], F32))
        wcv_pp = root.enter_context(nc.sbuf_tensor("wcv_pp", [128, NI * 4], F32))
        nw_pp = root.enter_context(nc.sbuf_tensor("nw_pp", [128, KD], F32))
        id_bf = root.enter_context(nc.sbuf_tensor("id_bf", [128, 128], BF16))
        cT_bf = root.enter_context(nc.sbuf_tensor("cT_bf", [128, KD * 8], BF16))
        brc = root.enter_context(nc.sbuf_tensor("brc", [128, 8], F32))
        cmb = root.enter_context(nc.sbuf_tensor("cmb", [128, 128], F32))
        carry = root.enter_context(nc.sbuf_tensor("carry", [128, NI * S], F32))
        wxp_bf = root.enter_context(nc.sbuf_tensor("wxp_bf", [128, NI * 96], BF16))
        wdt_bf = root.enter_context(nc.sbuf_tensor("wdt_bf", [DTR, DI], BF16))
        dt_bf = root.enter_context(nc.sbuf_tensor("dt_bf", [DTR, T], BF16))
        smll = root.enter_context(nc.sbuf_tensor("smll", [128, 64], F32))
        epsc = root.enter_context(nc.sbuf_tensor("epsc", [128, 1], F32))
        id32 = root.enter_context(nc.sbuf_tensor("id32", [128, 128], F32))

        ps = [root.enter_context(nc.psum_tensor(f"ps{i}", [128, 512], F32)) for i in range(6)]
        psT = root.enter_context(nc.psum_tensor("psT", [128, 1024], BF16))

        blk = root.enter_context(nc.Block())

        with ExitStack() as ph0:
            stg0 = ph0.enter_context(nc.sbuf_tensor("stg0", [128, 2048], F32))

            @blk.sync
            def _(q):
                for i in range(NI):
                    sQ.inc(q.dma_start(bdt_pp[:, i:i + 1], bdt[i]), 16)
                    sQ.inc(q.dma_start(bcv_pp[:, i:i + 1], bcv[i]), 16)
                    sQ.inc(q.dma_start(d_pp[:, i:i + 1], dvec[i]), 16)
                    sQ.inc(q.dma_start(wcv_pp[:, i * 4:(i + 1) * 4], wcv[i]), 16)
                    sQ.inc(q.dma_start(A_neg[:, i * S:(i + 1) * S], alog[i]), 16)
                for k in range(KD):
                    sQ.inc(q.dma_start(nw_pp[:, k:k + 1], normw[k]), 16)
                sQ.inc(q.dma_start(id_bf[:], ident[:]), 16)
                sQ.inc(q.dma_start(brc[:], biasr[0:1, :].partition_broadcast(128)), 16)
            c_const = sQ.n

            @blk.scalar
            def _(s):
                s.wait_ge(sQ.h, c_const)
                sS.inc(s.activation(A_neg[:], A_neg[:], AF.Exp))
            c_Aexp = sS.n

            @blk.vector
            def _(v):
                v.wait_ge(sS.h, c_Aexp)
                v.tensor_scalar_mul(A_neg[:], A_neg[:], -1.0)
                v.memset(carry[:], 0.0)
                v.memset(epsc[:], EPS)
                sV.inc(v.tensor_copy(id32[:], id_bf[:]))
            c_A = sV.n

            @blk.scalar
            def _(s):
                for i in range(NI):
                    sW.inc(s.dma_start(stg0[:, i * 96:(i + 1) * 96], wxp[i]), 16)
            c_ld1 = sW.n

            @blk.vector
            def _(v):
                v.wait_ge(sW.h, c_ld1)
                sV.inc(v.tensor_copy(wxp_bf[:], stg0[:, 0:NI * 96]))
            c_wxp = sV.n

            @blk.scalar
            def _(s):
                s.wait_ge(sV.h, c_wxp)
                sW.inc(s.dma_start(stg0[0:DTR, :], wdt[:]), 16)
            c_ld2 = sW.n

            @blk.vector
            def _(v):
                v.wait_ge(sW.h, c_ld2)
                sV.inc(v.tensor_copy(wdt_bf[:], stg0[0:DTR, :]))
            c_wdt = sV.n

            @blk.scalar
            def _(s):
                s.wait_ge(sV.h, c_wdt)
                for k in range(KD):
                    sW.inc(s.dma_start(stg0[:, k * 8:(k + 1) * 8], centT[k]), 16)
            c_ld3 = sW.n

            @blk.vector
            def _(v):
                v.wait_ge(sW.h, c_ld3)
                sV.inc(v.tensor_copy(cT_bf[:], stg0[:, 0:KD * 8]))
            c_cT = sV.n

        with ExitStack() as ph12:
            xnT = ph12.enter_context(nc.sbuf_tensor("xnT", [128, KD * T], BF16))
            winb = ph12.enter_context(nc.sbuf_tensor("winb", [128, KD * 4096], BF16))
            stg = ph12.enter_context(nc.sbuf_tensor("stg", [128, 4096], F32))
            xt = ph12.enter_context(nc.sbuf_tensor("xt", [128, DM], F32))
            xnb = ph12.enter_context(nc.sbuf_tensor("xnb", [128, DM], BF16))
            sq_t = ph12.enter_context(nc.sbuf_tensor("sq_t", [128, DM], F32))
            upre = [ph12.enter_context(nc.sbuf_tensor(f"upre{j}", [128, 2051], F32)) for j in range(2)]
            convt = [ph12.enter_context(nc.sbuf_tensor(f"convt{j}", [128, T], F32)) for j in range(2)]
            ubfT = [ph12.enter_context(nc.sbuf_tensor(f"ubfT{j}", [128, T], BF16)) for j in range(2)]
            resst = [ph12.enter_context(nc.sbuf_tensor(f"resst{j}", [128, T], F32)) for j in range(2)]

            win_cast = []
            for n in range(2 * KD):
                k, hh = divmod(n, 2)

                @blk.scalar
                def _(s, n=n, k=k, hh=hh):
                    if n >= 2:
                        s.wait_ge(sV.h, win_cast[n - 2])
                    cdma = sW.inc(s.dma_start(stg[:, (n % 2) * 2048:(n % 2 + 1) * 2048],
                                              w_in[k, :, hh * 2048:(hh + 1) * 2048]), 16)
                    win_ld.append(cdma)

                @blk.vector
                def _(v, n=n, k=k, hh=hh):
                    v.wait_ge(sW.h, win_ld[n])
                    c = sV.inc(v.tensor_scalar(winb[:, k * 4096 + hh * 2048:k * 4096 + (hh + 1) * 2048],
                                               stg[:, (n % 2) * 2048:(n % 2 + 1) * 2048],
                                               nw_pp[:, k:k + 1], None, ALU.mult))
                    win_cast.append(c)
            c_winb = sV.n

            for tc in range(T // 128):
                @blk.sync
                def _(q, tc=tc):
                    if tc >= 1:
                        q.wait_ge(sV.h, xn_done[tc - 1])
                    xld.append(sQ.inc(q.dma_start(xt[:], x_h[tc * 128:(tc + 1) * 128, :]), 16))

                @blk.scalar
                def _(s, tc=tc):
                    s.wait_ge(sQ.h, xld[tc])
                    c = sS.inc(s.activation(sq_t[:], xt[:], AF.Square))
                    sqt_done.append(c)

                @blk.vector
                def _(v, tc=tc):
                    v.wait_ge(sS.h, sqt_done[tc])
                    sV.inc(v.reduce_sum(smll[:, 0:1], sq_t[:], axis=AX.X))
                c_ms = sV.n

                @blk.scalar
                def _(s, tc=tc, c_ms=c_ms):
                    s.wait_ge(sV.h, c_ms)
                    s.activation(smll[:, 1:2], smll[:, 0:1], AF.Ln, bias=epsc[:, 0:1], scale=1.0 / DM)
                    sS.inc(s.activation(smll[:, 2:3], smll[:, 1:2], AF.Exp, scale=-0.5))
                c_rr = sS.n

                @blk.vector
                def _(v, tc=tc, c_rr=c_rr):
                    v.wait_ge(sS.h, c_rr)
                    if tc >= 1:
                        v.wait_ge(sT.h, xtr_done[tc - 1])
                    xn_done.append(sV.inc(v.tensor_scalar(xnb[:], xt[:], smll[:, 2:3], None, ALU.mult)))

                @blk.tensor
                def _(t, tc=tc):
                    t.wait_ge(sV.h, xn_done[tc])
                    if tc >= 1:
                        t.wait_ge(sV.h, xev_done[tc - 1])
                    ii = None
                    for j in range(KD):
                        ii = t.transpose(psT[:, j * 128:(j + 1) * 128],
                                         xnb[:, j * 128:(j + 1) * 128], id_bf[:])
                    xtr_done.append(sT.inc(ii))

                @blk.vector
                def _(v, tc=tc):
                    v.wait_ge(sT.h, xtr_done[tc])
                    ii = None
                    for j in range(KD):
                        ii = v.tensor_copy(xnT[:, j * T + tc * 128:j * T + (tc + 1) * 128],
                                           psT[:, j * 128:(j + 1) * 128])
                    xev_done.append(sV.inc(ii))
            c_xnT = sV.n

            ps_free = [0, 0, 0, 0]
            for mi in range(32):
                is_u = mi < 16
                i = mi if is_u else mi - 16

                @blk.tensor
                def _(t, mi=mi):
                    t.wait_ge(sV.h, c_xnT)
                    t.wait_ge(sV.h, c_winb)
                    for j in range(4):
                        if ps_free[j]:
                            t.wait_ge(sS.h, ps_free[j])
                    ii = None
                    for k in range(KD):
                        for nsr in range(4):
                            ii = t.matmul(ps[nsr][:],
                                          winb[:, k * 4096 + mi * 128:k * 4096 + (mi + 1) * 128],
                                          xnT[:, k * T + nsr * 512:k * T + (nsr + 1) * 512],
                                          start=(k == 0), stop=(k == KD - 1))
                    ip_mm.append(sT.inc(ii))
                c_mm = sT.n

                if is_u:
                    @blk.scalar
                    def _(s, i=i, c_mm=c_mm):
                        s.wait_ge(sT.h, c_mm)
                        if i >= 2:
                            s.wait_ge(sV.h, conv_done[i - 2])
                        s.memzero(upre[i % 2][:, 0:3])
                        ii = None
                        for nsr in range(4):
                            ii = s.copy(upre[i % 2][:, 3 + nsr * 512:3 + (nsr + 1) * 512], ps[nsr][:])
                        c = sS.inc(ii)
                        for j in range(4):
                            ps_free[j] = c
                        uev_done.append(c)

                    @blk.vector
                    def _(v, i=i):
                        v.wait_ge(sS.h, uev_done[i])
                        if i >= 2:
                            v.wait_ge(sG.h, uspill_done[i - 2])
                        v.tensor_scalar(convt[i % 2][:], upre[i % 2][:, 0:T],
                                        wcv_pp[:, i * 4:i * 4 + 1], None, ALU.mult)
                        for kk in range(1, 4):
                            v.scalar_tensor_tensor(convt[i % 2][:], upre[i % 2][:, kk:kk + T],
                                                   wcv_pp[:, i * 4 + kk:i * 4 + kk + 1],
                                                   convt[i % 2][:], ALU.mult, ALU.add)
                        conv_done.append(sV.inc(v.engine_nop()))

                    @blk.scalar
                    def _(s, i=i):
                        s.wait_ge(sV.h, conv_done[i])
                        silu_done.append(sS.inc(s.activation(convt[i % 2][:], convt[i % 2][:],
                                                             AF.Silu, bias=bcv_pp[:, i:i + 1])))

                    @blk.vector
                    def _(v, i=i):
                        v.wait_ge(sS.h, silu_done[i])
                        if i >= 2:
                            v.wait_ge(sG.h, uspill_done[i - 2])
                        ubf_done.append(sV.inc(v.tensor_copy(ubfT[i % 2][:], convt[i % 2][:])))

                    @blk.gpsimd
                    def _(g, i=i):
                        g.wait_ge(sV.h, ubf_done[i])
                        sG.inc(g.dma_start(u_s[i], convt[i % 2][:]), 16)
                        uspill_done.append(sG.inc(g.dma_start(ub_s[i], ubfT[i % 2][:]), 16))
                else:
                    @blk.scalar
                    def _(s, i=i, c_mm=c_mm):
                        s.wait_ge(sT.h, c_mm)
                        if i >= 2:
                            s.wait_ge(sG.h, rspill_done[i - 2])
                        ii = None
                        for nsr in range(4):
                            ii = s.copy(resst[i % 2][:, nsr * 512:(nsr + 1) * 512], ps[nsr][:])
                        c = sS.inc(ii)
                        for j in range(4):
                            ps_free[j] = c
                        rev_done.append(c)

                    @blk.gpsimd
                    def _(g, i=i):
                        g.wait_ge(sS.h, rev_done[i])
                        rspill_done.append(sG.inc(g.dma_start(res_s[i], resst[i % 2][:]), 16))
            c_ph2_g = sG.n
            c_ph2_s = sS.n

        with ExitStack() as ph3:
          if phases >= 3:
              ubm = [ph3.enter_context(nc.sbuf_tensor(f"ubm{j}", [128, 512], BF16)) for j in range(3)]
              dtb_sb = ph3.enter_context(nc.sbuf_tensor("dtb_sb", [80, T], F32))
              c_sb = ph3.enter_context(nc.sbuf_tensor("c_sb", [16, T], F32))

              for nsr in range(4):
                  for k in range(NI):
                      n = nsr * NI + k

                      @blk.sync
                      def _(q, nsr=nsr, k=k, n=n):
                          q.wait_ge(sG.h, c_ph2_g)
                          if n >= 3:
                              q.wait_ge(sT.h, xp_mm[n - 3])
                          ub_ld.append(sQ.inc(q.dma_start(ubm[n % 3][:],
                                                          ub_s[k, :, nsr * 512:(nsr + 1) * 512]), 16))

                      @blk.tensor
                      def _(t, nsr=nsr, k=k, n=n):
                          if n == 0:
                              t.wait_ge(sV.h, c_wxp)
                          if k == 0 and nsr >= 1:
                              t.wait_ge(sS.h, xp_ev[nsr - 1])
                          t.wait_ge(sQ.h, ub_ld[n])
                          t.matmul(ps[0][0:80, :], wxp_bf[:, k * 96:k * 96 + 80], ubm[n % 3][:],
                                   start=(k == 0), stop=(k == NI - 1))
                          ii = t.matmul(ps[1][0:16, :], wxp_bf[:, k * 96 + 80:k * 96 + 96], ubm[n % 3][:],
                                        start=(k == 0), stop=(k == NI - 1))
                          xp_mm.append(sT.inc(ii))

                  @blk.scalar
                  def _(s, nsr=nsr):
                      s.wait_ge(sT.h, xp_mm[-1])
                      s.copy(dtb_sb[:, nsr * 512:(nsr + 1) * 512], ps[0][0:80, :])
                      xp_ev.append(sS.inc(s.copy(c_sb[:, nsr * 512:(nsr + 1) * 512], ps[1][0:16, :])))
              c_xp = sS.n

              @blk.vector
              def _(v):
                  v.wait_ge(sS.h, c_xp)
                  sV.inc(v.tensor_copy(dt_bf[:], dtb_sb[0:DTR, :]))
              c_dtbf = sV.n

              @blk.gpsimd
              def _(g):
                  g.wait_ge(sS.h, c_xp)
                  sG.inc(g.dma_start(bc_s[0:16, :], dtb_sb[64:80, :]), 16)
                  sG.inc(g.dma_start(bc_s[16:32, :], c_sb[:]), 16)
                  if debug:
                      sG.inc(g.dma_start(dbg_dt[:], dtb_sb[0:DTR, :]), 16)
              c_bcs = sG.n

        B3 = snap()
        with ExitStack() as ph4:
          if phases >= 4:
              Bbc = ph4.enter_context(nc.sbuf_tensor("Bbc", [128, S * TQ], F32))
              Cbc = ph4.enter_context(nc.sbuf_tensor("Cbc", [128, S * TQ], F32))
              delta = ph4.enter_context(nc.sbuf_tensor("delta", [128, TQ], F32))
              te_b = ph4.enter_context(nc.sbuf_tensor("te_b", [128, TQ], F32))
              w_t = ph4.enter_context(nc.sbuf_tensor("w_t", [128, TQ], F32))
              u_t = [ph4.enter_context(nc.sbuf_tensor(f"u_t{j}", [128, TQ], F32)) for j in range(2)]
              res_t = [ph4.enter_context(nc.sbuf_tensor(f"res_t{j}", [128, TQ], F32)) for j in range(2)]
              a_b = [ph4.enter_context(nc.sbuf_tensor(f"a_b{j}", [128, TQ], F32)) for j in range(2)]
              b_b = [ph4.enter_context(nc.sbuf_tensor(f"b_b{j}", [128, TQ], F32)) for j in range(2)]
              h_b = ph4.enter_context(nc.sbuf_tensor("h_b", [128, TQ], F32))
              p_b = [ph4.enter_context(nc.sbuf_tensor(f"p_b{j}", [128, TQ], F32)) for j in range(3)]
              y2 = ph4.enter_context(nc.sbuf_tensor("y2", [128, TQ], F32))
              sr_b = ph4.enter_context(nc.sbuf_tensor("sr_b", [128, TQ], F32))
              ybf = ph4.enter_context(nc.sbuf_tensor("ybf", [128, TQ], BF16))

              for q4 in range(4):
                  @blk.sync
                  def _(q, q4=q4):
                      if q4 == 0:
                          bar(q, B3)
                      q.wait_ge(sG.h, c_bcs)
                      if q4 > 0:
                          q.wait_ge(sV.h, q_last_p[q4 - 1])
                          q.wait_ge(sP.h, q_last_b[q4 - 1])
                      for s_ in range(S):
                          sQ.inc(q.dma_start(Bbc[:, s_ * TQ:(s_ + 1) * TQ],
                                             bc_s[s_:s_ + 1, q4 * TQ:(q4 + 1) * TQ]
                                             .partition_broadcast(128)), 16)
                          sQ.inc(q.dma_start(Cbc[:, s_ * TQ:(s_ + 1) * TQ],
                                             bc_s[16 + s_:17 + s_, q4 * TQ:(q4 + 1) * TQ]
                                             .partition_broadcast(128)), 16)
                  c_bc = sQ.n

                  for i in range(NI):
                      gi = q4 * NI + i

                      @blk.scalar
                      def _(s, q4=q4, i=i, gi=gi):
                          if gi == 0:
                              bar(s, B3)
                          if gi >= 2:
                              s.wait_ge(sV.h, ut_free[gi - 2])
                              s.wait_ge(sS.h, sr_done[gi - 2])
                          ur_ld.append((sW.inc(s.dma_start(u_t[i % 2][:],
                                                           u_s[i, :, q4 * TQ:(q4 + 1) * TQ]), 16),
                                        sW.inc(s.dma_start(res_t[i % 2][:],
                                                           res_s[i, :, q4 * TQ:(q4 + 1) * TQ]), 16)))

                      @blk.tensor
                      def _(t, q4=q4, i=i, gi=gi):
                          if gi == 0:
                              bar(t, B3)
                          t.wait_ge(sV.h, c_dtbf)
                          if gi >= 1:
                              t.wait_ge(sS.h, dl_done[gi - 1])
                          ii = t.matmul(ps[4][:], wdt_bf[:, i * 128:(i + 1) * 128],
                                        dt_bf[:, q4 * TQ:(q4 + 1) * TQ], start=True, stop=True)
                          dtm_done.append(sT.inc(ii))

                      @blk.scalar
                      def _(s, q4=q4, i=i, gi=gi):
                          s.wait_ge(sT.h, dtm_done[gi])
                          if gi >= 1:
                              s.wait_ge(sV.h, w_done[gi - 1])
                          s.activation(te_b[:], ps[4][:], AF.Exp, bias=bdt_pp[:, i:i + 1])
                          dl_done.append(sS.inc(s.activation(delta[:], te_b[:], AF.Ln, bias=1.0)))

                      @blk.vector
                      def _(v, q4=q4, i=i, gi=gi):
                          if gi == 0:
                              bar(v, B3)
                          v.wait_ge(sS.h, dl_done[gi])
                          v.wait_ge(sW.h, ur_ld[gi][0])
                          if b_rdy:
                              v.wait_ge(sP.h, b_rdy[-1])
                          w_done.append(sV.inc(v.tensor_mul(w_t[:], delta[:], u_t[i % 2][:])))

                      for s_ in range(S):
                          nb = s_ % 2

                          @blk.scalar
                          def _(s, q4=q4, i=i, s_=s_, nb=nb, gi=gi):
                              s.wait_ge(sS.h, dl_done[gi])
                              if len(scan_done) >= 2:
                                  s.wait_ge(sV.h, scan_done[-2])
                              a_rdy.append(sS.inc(s.activation(a_b[nb][:], delta[:], AF.Exp,
                                                               scale=A_neg[:, i * S + s_:i * S + s_ + 1])))

                          @blk.gpsimd
                          def _(g, q4=q4, i=i, s_=s_, nb=nb, gi=gi, c_bc=c_bc):
                              if gi == 0 and s_ == 0:
                                  bar(g, B3)
                              g.wait_ge(sV.h, w_done[gi])
                              g.wait_ge(sQ.h, c_bc)
                              if len(scan_done) >= 2:
                                  g.wait_ge(sV.h, scan_done[-2])
                              b_rdy.append(sP.inc(g.tensor_tensor(b_b[nb][:], w_t[:],
                                                                  Bbc[:, s_ * TQ:(s_ + 1) * TQ],
                                                                  ALU.mult)))

                          @blk.vector
                          def _(v, q4=q4, i=i, s_=s_, nb=nb, gi=gi):
                              v.wait_ge(sS.h, a_rdy[-1])
                              v.wait_ge(sP.h, b_rdy[-1])
                              init = 0.0 if q4 == 0 else carry[:, i * S + s_:i * S + s_ + 1]
                              ii = v.tensor_tensor_scan(h_b[:], a_b[nb][:], b_b[nb][:], init,
                                                        ALU.mult, ALU.add)
                              scan_done.append(sV.inc(ii))
                              v.wait_ge(sV.h, scan_done[-1])
                              if q4 < 3:
                                  v.tensor_copy(carry[:, i * S + s_:i * S + s_ + 1], h_b[:, TQ - 1:TQ])
                              if len(y_mm) >= 3:
                                  v.wait_ge(sT.h, y_mm[-3])
                              p_rdy.append(sV.inc(v.tensor_tensor(p_b[s_ % 3][:], h_b[:],
                                                                  Cbc[:, s_ * TQ:(s_ + 1) * TQ],
                                                                  ALU.mult)))

                          @blk.tensor
                          def _(t, q4=q4, i=i, s_=s_, gi=gi):
                              t.wait_ge(sV.h, p_rdy[-1])
                              if s_ == 0 and gi >= 1:
                                  t.wait_ge(sV.h, ut_free[gi - 1])
                              y_mm.append(sT.inc(t.matmul(ps[5][:], id32[:], p_b[s_ % 3][:],
                                                          start=(s_ == 0), stop=(s_ == S - 1))))
                      q_last_b.append(b_rdy[-1])

                      @blk.scalar
                      def _(s, q4=q4, i=i, gi=gi):
                          s.wait_ge(sW.h, ur_ld[gi][1])
                          if gi >= 1:
                              s.wait_ge(sV.h, ybf_done[gi - 1])
                          sr_done.append(sS.inc(s.activation(sr_b[:], res_t[i % 2][:], AF.Silu)))

                      @blk.vector
                      def _(v, q4=q4, i=i, gi=gi):
                          v.wait_ge(sT.h, y_mm[-1])
                          v.wait_ge(sS.h, sr_done[gi])
                          if gi >= 1:
                              v.wait_ge(sG.h, ysp_done[gi - 1])
                          ut_free.append(sV.inc(v.scalar_tensor_tensor(y2[:], u_t[i % 2][:],
                                                                       d_pp[:, i:i + 1], ps[5][:],
                                                                       ALU.mult, ALU.add)))
                          v.wait_ge(sV.h, ut_free[-1])
                          ybf_done.append(sV.inc(v.tensor_mul(ybf[:], y2[:], sr_b[:])))

                      @blk.gpsimd
                      def _(g, q4=q4, i=i, gi=gi):
                          g.wait_ge(sV.h, ybf_done[gi])
                          ysp_done.append(sG.inc(g.dma_start(y_s[i, :, q4 * TQ:(q4 + 1) * TQ],
                                                             ybf[:]), 16))
                  q_last_p.append(p_rdy[-1])
        c_scan_g = sG.n
        B4 = snap()

        late = ExitStack()
        x2T = late.enter_context(nc.sbuf_tensor("x2T", [128, KD * T], BF16))
        sq5 = late.enter_context(nc.sbuf_tensor("sq5", [128, DM], F32))

        with ExitStack() as ph5:
          if phases >= 5:
              woutb   = ph5.enter_context(nc.sbuf_tensor("woutb", [128, NI * DM], BF16))
              stg5 = [ph5.enter_context(nc.sbuf_tensor(f"stg5{j}", [128, DM], F32)) for j in range(2)]
              yst = [ph5.enter_context(nc.sbuf_tensor(f"yst{j}", [128, T], BF16)) for j in range(2)]
              xres = [ph5.enter_context(nc.sbuf_tensor(f"xres{j}", [128, DM], F32)) for j in range(2)]
              x2p = ph5.enter_context(nc.sbuf_tensor("x2p", [128, DM], F32))
              x2f = [ph5.enter_context(nc.sbuf_tensor(f"x2f{j}", [128, DM], F32)) for j in range(2)]
              x2b = ph5.enter_context(nc.sbuf_tensor("x2b", [128, DM], BF16))
              raw = ph5.enter_context(nc.sbuf_tensor("raw", [128, 8], F32))
              tk = ph5.enter_context(nc.sbuf_tensor("tk", [128, 64], F32))

              for k in range(NI):
                  @blk.scalar
                  def _(s, k=k):
                      if k == 0:
                          bar(s, B4)
                      if k >= 2:
                          s.wait_ge(sV.h, wo_cast[k - 2])
                      wo_ld.append(sW.inc(s.dma_start(stg5[k % 2][:], wout[k]), 16))

                  @blk.vector
                  def _(v, k=k):
                      if k == 0:
                          bar(v, B4)
                      v.wait_ge(sW.h, wo_ld[k])
                      wo_cast.append(sV.inc(v.tensor_copy(woutb[:, k * DM:(k + 1) * DM],
                                                          stg5[k % 2][:])))
              c_wo = sV.n

              for tc in range(T // 128):
                  @blk.sync
                  def _(q, tc=tc):
                      if tc == 0:
                          bar(q, B4)
                      if tc >= 2:
                          q.wait_ge(sT.h, op_mm[tc - 2])
                          q.wait_ge(sV.h, x2pre_done[tc - 2])
                      for k in range(NI):
                          sQ.inc(q.dma_start(yst[tc % 2][:, k * 128:(k + 1) * 128],
                                             y_s[k, :, tc * 128:(tc + 1) * 128]), 16)
                      yld.append(sQ.n)
                      xrld.append(sQ.inc(q.dma_start(xres[tc % 2][:],
                                                     x_h[tc * 128:(tc + 1) * 128, :]), 16))

                  @blk.tensor
                  def _(t, tc=tc):
                      if tc == 0:
                          bar(t, B4)
                      t.wait_ge(sV.h, c_wo)
                      t.wait_ge(sQ.h, yld[tc])
                      if tc >= 1:
                          t.wait_ge(sV.h, x2pre_done[tc - 1])
                      ii = None
                      for k in range(NI):
                          stt = yst[tc % 2][:, k * 128:(k + 1) * 128]
                          t.matmul(ps[0][:], stt, woutb[:, k * DM:k * DM + 512],
                                   start=(k == 0), stop=(k == NI - 1))
                          ii = t.matmul(ps[1][:], stt, woutb[:, k * DM + 512:(k + 1) * DM],
                                        start=(k == 0), stop=(k == NI - 1))
                      op_mm.append(sT.inc(ii))

                  @blk.vector
                  def _(v, tc=tc):
                      v.wait_ge(sT.h, op_mm[tc])
                      v.wait_ge(sQ.h, xrld[tc])
                      v.tensor_tensor(x2p[:, 0:512], ps[0][:], xres[tc % 2][:, 0:512], ALU.add)
                      x2pre_done.append(sV.inc(v.tensor_tensor(x2p[:, 512:1024], ps[1][:],
                                                               xres[tc % 2][:, 512:1024], ALU.add)))
                      v.wait_ge(sV.h, x2pre_done[-1])
                      sV.inc(v.reduce_sum(smll[:, 4:5], x2p[:], axis=AX.X))

                  @blk.scalar
                  def _(s, tc=tc):
                      s.wait_ge(sV.h, x2pre_done[tc])
                      sq5_done.append(sS.inc(s.activation(sq5[:], x2p[:], AF.Square)))

                  @blk.vector
                  def _(v, tc=tc):
                      v.wait_ge(sS.h, sq5_done[-1])
                      v.reduce_sum(smll[:, 5:6], sq5[:], axis=AX.X)
                      c1 = sV.inc(v.tensor_scalar_mul(smll[:, 4:5], smll[:, 4:5], 1.0 / DM))
                      v.wait_ge(sV.h, c1)
                      c2 = sV.inc(v.tensor_mul(smll[:, 6:7], smll[:, 4:5], smll[:, 4:5]))
                      v.wait_ge(sV.h, c2)
                      sV.inc(v.scalar_tensor_tensor(smll[:, 5:6], smll[:, 5:6], 1.0 / DM,
                                                    smll[:, 6:7], ALU.mult, ALU.subtract))
                  c_var = sV.n

                  @blk.scalar
                  def _(s, tc=tc, c_var=c_var):
                      s.wait_ge(sV.h, c_var)
                      s.activation(smll[:, 7:8], smll[:, 5:6], AF.Ln, bias=epsc[:, 0:1])
                      sS.inc(s.activation(smll[:, 8:9], smll[:, 7:8], AF.Exp, scale=-0.5))
                  c_rstd = sS.n

                  @blk.vector
                  def _(v, tc=tc, c_rstd=c_rstd):
                      v.wait_ge(sS.h, c_rstd)
                      if tc >= 2:
                          v.wait_ge(sG.h, x2sp_done[tc - 2])
                      cxf = sV.inc(v.tensor_scalar(x2f[tc % 2][:], x2p[:], smll[:, 4:5],
                                                   smll[:, 8:9], ALU.subtract, ALU.mult))
                      v.wait_ge(sV.h, cxf)
                      x2_done.append(sV.inc(v.tensor_copy(x2b[:], x2f[tc % 2][:])))

                  @blk.gpsimd
                  def _(g, tc=tc):
                      if tc == 0:
                          bar(g, B4)
                      g.wait_ge(sV.h, x2_done[tc])
                      x2sp_done.append(sG.inc(g.dma_start(x2_s[tc * 128:(tc + 1) * 128, :],
                                                          x2f[tc % 2][:]), 16))

                  @blk.tensor
                  def _(t, tc=tc):
                      t.wait_ge(sV.h, x2_done[tc])
                      if tc >= 1:
                          t.wait_ge(sV.h, x2ev_done[tc - 1])
                      ii = None
                      for j in range(KD):
                          ii = t.transpose(psT[:, j * 128:(j + 1) * 128],
                                           x2b[:, j * 128:(j + 1) * 128], id_bf[:])
                      x2tr_done.append(sT.inc(ii))

                  @blk.vector
                  def _(v, tc=tc):
                      v.wait_ge(sT.h, x2tr_done[tc])
                      ii = None
                      for j in range(KD):
                          ii = v.tensor_copy(x2T[:, j * T + tc * 128:j * T + (tc + 1) * 128],
                                             psT[:, j * 128:(j + 1) * 128])
                      x2ev_done.append(sV.inc(ii))

                  @blk.tensor
                  def _(t, tc=tc):
                      t.wait_ge(sV.h, x2ev_done[tc])
                      t.wait_ge(sV.h, c_cT)
                      if tc >= 1:
                          t.wait_ge(sV.h, rt_ev[tc - 1])
                      ii = None
                      for k in range(KD):
                          ii = t.matmul(ps[5][:, 0:8],
                                        x2T[:, k * T + tc * 128:k * T + (tc + 1) * 128],
                                        cT_bf[:, k * 8:(k + 1) * 8],
                                        start=(k == 0), stop=(k == KD - 1))
                      rt_mm.append(sT.inc(ii))

                  @blk.vector
                  def _(v, tc=tc):
                      v.wait_ge(sT.h, rt_mm[tc])
                      rt_ev.append(sV.inc(v.tensor_copy(raw[:], ps[5][:, 0:8])))

                  @blk.scalar
                  def _(s, tc=tc):
                      s.wait_ge(sV.h, rt_ev[tc])
                      sS.inc(s.activation(tk[:, 0:8], raw[:], AF.Sigmoid))
                  c_sc = sS.n

                  @blk.vector
                  def _(v, tc=tc, c_sc=c_sc):
                      v.wait_ge(sS.h, c_sc)
                      c = sV.inc(v.tensor_tensor(tk[:, 8:16], raw[:], brc[:], ALU.add))
                      v.wait_ge(sV.h, c)
                      c = sV.inc(v.reduce_max(tk[:, 16:17], tk[:, 8:16], axis=AX.X))
                      v.wait_ge(sV.h, c)
                      c = sV.inc(v.tensor_scalar(tk[:, 24:32], tk[:, 8:16], tk[:, 16:17],
                                                 None, ALU.is_ge))
                      v.wait_ge(sV.h, c)
                      c = sV.inc(v.tensor_mul(tk[:, 32:40], tk[:, 24:32], tk[:, 0:8]))
                      v.wait_ge(sV.h, c)
                      c = sV.inc(v.reduce_sum(tk[:, 17:18], tk[:, 32:40], axis=AX.X))
                      c = sV.inc(v.scalar_tensor_tensor(tk[:, 48:56], tk[:, 24:32], NEG_BIG,
                                                        tk[:, 8:16], ALU.mult, ALU.add))
                      v.wait_ge(sV.h, c)
                      c = sV.inc(v.reduce_max(tk[:, 18:19], tk[:, 48:56], axis=AX.X))
                      v.wait_ge(sV.h, c)
                      c = sV.inc(v.tensor_scalar(tk[:, 40:48], tk[:, 48:56], tk[:, 18:19],
                                                 None, ALU.is_ge))
                      v.wait_ge(sV.h, c)
                      c = sV.inc(v.tensor_mul(tk[:, 56:64], tk[:, 40:48], tk[:, 0:8]))
                      v.wait_ge(sV.h, c)
                      c = sV.inc(v.reduce_sum(tk[:, 19:20], tk[:, 56:64], axis=AX.X))
                      v.wait_ge(sV.h, c)
                      sV.inc(v.tensor_tensor(tk[:, 20:21], tk[:, 17:18], tk[:, 19:20],
                                             ALU.subtract))
                  c_d12 = sV.n

                  @blk.scalar
                  def _(s, tc=tc, c_d12=c_d12):
                      s.wait_ge(sV.h, c_d12)
                      sS.inc(s.activation(tk[:, 21:22], tk[:, 20:21], AF.Sigmoid))
                  c_w1 = sS.n

                  @blk.vector
                  def _(v, tc=tc, c_w1=c_w1):
                      v.wait_ge(sS.h, c_w1)
                      c1 = sV.inc(v.tensor_scalar(tk[:, 22:23], tk[:, 21:22], -1.0, 1.0,
                                                  ALU.mult, ALU.add))
                      c2 = sV.inc(v.tensor_scalar(cmb[:, tc * 8:(tc + 1) * 8], tk[:, 24:32],
                                                  tk[:, 21:22], None, ALU.mult))
                      v.wait_ge(sV.h, c1)
                      v.wait_ge(sV.h, c2)
                      cmb_done.append(sV.inc(v.scalar_tensor_tensor(
                          cmb[:, tc * 8:(tc + 1) * 8], tk[:, 40:48], tk[:, 22:23],
                          cmb[:, tc * 8:(tc + 1) * 8], ALU.mult, ALU.add)))
              c_ph5_v = sV.n
              c_ph5_g = sG.n
              B5.update(snap())

              if debug:
                  @blk.gpsimd
                  def _(g):
                      g.wait_ge(sV.h, c_ph5_v)
                      sG.inc(g.dma_start(dbg_cmb[:], cmb[:]), 16)

        with ExitStack() as ph7:
          if phases >= 7:
              slotF   = ph7.enter_context(nc.sbuf_tensor("slotF", [128, KD * 2048], BF16))
              slotP = ph7.enter_context(nc.sbuf_tensor("slotP", [128, NFH * DM], BF16))
              stg7 = [ph7.enter_context(nc.sbuf_tensor(f"stg7{j}", [128, 2048], F32)) for j in range(2)]
              acc = ph7.enter_context(nc.sbuf_tensor("acc", [128, 4 * DM], F32))
              h_q = ph7.enter_context(nc.sbuf_tensor("h_q", [128, NFH * 512], BF16))
              outb = [ph7.enter_context(nc.sbuf_tensor(f"outb{j}", [128, DM], F32)) for j in range(2)]

              def load_half(kind, e, wh, free_val):
                  ntile = KD if kind == "fc" else NFH
                  for j in range(ntile):
                      @blk.scalar
                      def _(s, j=j, kind=kind, e=e, wh=wh, free_val=free_val):
                          if j == 0 and not w7_cast:
                              bar(s, B5)
                          if j == 0:
                              s.wait_ge(sT.h, free_val)
                          if len(w7_cast) >= 2:
                              s.wait_ge(sS.h, w7_cast[-2])
                          if kind == "fc":
                              src = wfcT[e, j * 128:(j + 1) * 128, wh * 2048:(wh + 1) * 2048]
                              dst = stg7[j % 2][:]
                          else:
                              src = wprT[e, wh * 2048 + j * 128:wh * 2048 + (j + 1) * 128, :]
                              dst = stg7[j % 2][:, 0:DM]
                          c = sW.inc(s.dma_start(dst, src), 16)
                          s.wait_ge(sW.h, c)
                          if kind == "fc":
                              ii = s.activation(slotF[:, j * 2048:(j + 1) * 2048],
                                                stg7[j % 2][:], AF.Copy)
                          else:
                              ii = s.activation(slotP[:, j * DM:(j + 1) * DM],
                                                stg7[j % 2][:, 0:DM], AF.Copy)
                          w7_cast.append(sS.inc(ii))
                  return w7_cast[-1]

              for ip in range(4):
                  t0 = ip * TQ
                  c0 = ip * 4

                  @blk.sync
                  def _(q, ip=ip, t0=t0, c0=c0):
                      if ip == 0:
                          bar(q, B5)
                      if ip > 0:
                          q.wait_ge(sG.h, o_store[-1])
                      for c in range(4):
                          sQ.inc(q.dma_start(acc[:, c * DM:(c + 1) * DM],
                                             x2_s[(c0 + c) * 128:(c0 + c + 1) * 128, :]), 16)
                      acc_ld.append(sQ.n)

                  @blk.vector
                  def _(v, ip=ip):
                      if ip == 0:
                          bar(v, B5)
                      v.wait_ge(sQ.h, acc_ld[ip])
                      ii = None
                      for c in range(4):
                          ii = v.tensor_scalar_mul(acc[:, c * DM:(c + 1) * DM],
                                                   acc[:, c * DM:(c + 1) * DM], 2.0)
                      acc_rdy.append(sV.inc(ii))

                  for e in range(NE):
                      for wh in range(2):
                          cF = load_half("fc", e, wh, fc_mm[-1] if fc_mm else 0)

                          for fl in range(NFH):
                              @blk.tensor
                              def _(t, ip=ip, e=e, wh=wh, fl=fl, t0=t0, cF=cF):
                                  if fl == 0:
                                      t.wait_ge(sS.h, cF)
                                      t.wait_ge(sV.h, acc_rdy[ip])
                                  if psf_last.get(fl % 2):
                                      t.wait_ge(sS.h, psf_last[fl % 2])
                                  if fl < 2 and psp_last.get(fl % 2):
                                      t.wait_ge(sV.h, psp_last[fl % 2])
                                  ii = None
                                  for k in range(KD):
                                      ii = t.matmul(ps[2 + fl % 2][:],
                                                    slotF[:, k * 2048 + fl * 128:k * 2048 + (fl + 1) * 128],
                                                    x2T[:, k * T + t0:k * T + t0 + TQ],
                                                    start=(k == 0), stop=(k == KD - 1))
                                  fc_at[(ip, e, wh, fl)] = sT.inc(ii)
                                  fc_mm.append(fc_at[(ip, e, wh, fl)])

                              @blk.scalar
                              def _(s, ip=ip, e=e, wh=wh, fl=fl):
                                  s.wait_ge(sT.h, fc_at[(ip, e, wh, fl)])
                                  c = sS.inc(s.activation(h_q[:, fl * 512:(fl + 1) * 512],
                                                          ps[2 + fl % 2][:], AF.Gelu))
                                  gel_done.append(c)
                                  gel_at[(ip, e, wh, fl)] = c
                                  psf_last[fl % 2] = c
                          cP = load_half("pr", e, wh, pr_mm[-1] if pr_mm else 0)
                          c_gel = gel_at[(ip, e, wh, NFH - 1)]

                          for sub in range(4):
                              @blk.tensor
                              def _(t, ip=ip, e=e, wh=wh, sub=sub, cP=cP, c_gel=c_gel):
                                  if sub == 0:
                                      t.wait_ge(sS.h, cP)
                                      t.wait_ge(sS.h, c_gel)
                                  if sub < 2 and psf_last.get(sub % 2):
                                      t.wait_ge(sS.h, psf_last[sub % 2])
                                  if psp_last.get(sub % 2):
                                      t.wait_ge(sV.h, psp_last[sub % 2])
                                  ii = None
                                  for fl in range(NFH):
                                      for dh in range(2):
                                          ii = t.matmul(ps[2 + 2 * (sub % 2) + dh][:],
                                                        h_q[:, fl * 512 + sub * 128:fl * 512 + (sub + 1) * 128],
                                                        slotP[:, fl * DM + dh * 512:fl * DM + (dh + 1) * 512],
                                                        start=(fl == 0), stop=(fl == NFH - 1))
                                  pr_at[(ip, e, wh, sub)] = sT.inc(ii)
                                  pr_mm.append(pr_at[(ip, e, wh, sub)])

                              @blk.vector
                              def _(v, ip=ip, e=e, wh=wh, sub=sub, c0=c0):
                                  v.wait_ge(sT.h, pr_at[(ip, e, wh, sub)])
                                  tc_g = c0 + sub
                                  ii = None
                                  for dh in range(2):
                                      sl = acc[:, sub * DM + dh * 512:sub * DM + (dh + 1) * 512]
                                      sc = 1.0 if e < 2 else cmb[:, tc_g * 8 + (e - 2):tc_g * 8 + (e - 1)]
                                      ii = v.scalar_tensor_tensor(sl, ps[2 + 2 * (sub % 2) + dh][:],
                                                                  sc, sl, ALU.mult, ALU.add)
                                  c = sV.inc(ii)
                                  pr_ev.append(c)
                                  psp_last[sub % 2] = c

                  for c in range(4):
                      @blk.vector
                      def _(v, ip=ip, c=c):
                          v.wait_ge(sV.h, pr_ev[-1])
                          sV.inc(v.reduce_sum(smll[:, 12:13], acc[:, c * DM:(c + 1) * DM], axis=AX.X))
                      c_m2 = sV.n

                      @blk.scalar
                      def _(s, ip=ip, c=c, c_m2=c_m2):
                          s.wait_ge(sV.h, c_m2)
                          sS.inc(s.activation(sq5[:], acc[:, c * DM:(c + 1) * DM], AF.Square))
                      c_sq2 = sS.n

                      @blk.vector
                      def _(v, ip=ip, c=c, c_sq2=c_sq2):
                          v.wait_ge(sS.h, c_sq2)
                          v.reduce_sum(smll[:, 13:14], sq5[:], axis=AX.X)
                          c1 = sV.inc(v.tensor_scalar_mul(smll[:, 12:13], smll[:, 12:13], 1.0 / DM))
                          v.wait_ge(sV.h, c1)
                          c2 = sV.inc(v.tensor_mul(smll[:, 14:15], smll[:, 12:13], smll[:, 12:13]))
                          v.wait_ge(sV.h, c2)
                          sV.inc(v.scalar_tensor_tensor(smll[:, 13:14], smll[:, 13:14], 1.0 / DM,
                                                        smll[:, 14:15], ALU.mult, ALU.subtract))
                      c_v2 = sV.n

                      @blk.scalar
                      def _(s, ip=ip, c=c, c_v2=c_v2):
                          s.wait_ge(sV.h, c_v2)
                          s.activation(smll[:, 15:16], smll[:, 13:14], AF.Ln, bias=epsc[:, 0:1])
                          sS.inc(s.activation(smll[:, 16:17], smll[:, 15:16], AF.Exp, scale=-0.5))
                      c_r2 = sS.n

                      @blk.vector
                      def _(v, ip=ip, c=c, c_r2=c_r2):
                          v.wait_ge(sS.h, c_r2)
                          nb = len(o_done) % 2
                          if len(o_done) >= 2:
                              v.wait_ge(sG.h, o_store[-2])
                          o_done.append(sV.inc(v.tensor_scalar(outb[nb][:],
                                                               acc[:, c * DM:(c + 1) * DM],
                                                               smll[:, 12:13], smll[:, 16:17],
                                                               ALU.subtract, ALU.mult)))

                      @blk.gpsimd
                      def _(g, ip=ip, c=c, c0=c0):
                          if ip == 0 and c == 0:
                              bar(g, B5)
                          g.wait_ge(sV.h, o_done[-1])
                          nb = (len(o_done) - 1) % 2
                          o_store.append(sG.inc(g.dma_start(out_h[(c0 + c) * 128:(c0 + c + 1) * 128, :],
                                                            outb[nb][:]), 16))

              @blk.gpsimd
              def _(g):
                  g.wait_ge(sG.h, sG.n)
        late.close()
    return nc


win_ld = []
win_cast = []
xld = []
sqt_done = []
xn_done = []
xtr_done = []
xev_done = []
ip_mm = []
uev_done = []
conv_done = []
silu_done = []
ubf_done = []
uspill_done = []
rev_done = []
rspill_done = []
ub_ld = []
xp_mm = []
xp_ev = []
q_last_b = []
q_last_p = []
ur_ld = []
dtm_done = []
dl_done = []
w_done = []
wb_last = []
a_rdy = []
b_rdy = []
scan_done = []
p_rdy = []
acc_dma = []
y_mm = []
ut_free = []
sr_done = []
ybf_done = []
ysp_done = []
wo_ld = []
wo_cast = []
yld = []
xrld = []
op_mm = []
x2pre_done = []
sq5_done = []
x2_done = []
x2sp_done = []
x2tr_done = []
x2ev_done = []
rt_mm = []
rt_ev = []
cmb_done = []
w7_cast = []
fc_mm = []
fc_at = {}
gel_done = []
gel_at = {}
pr_mm = []
pr_at = {}
pr_ev = []
prev_ev = {}
psf_last = {}
psp_last = {}
acc_ld = []
acc_rdy = []
o_done = []
o_store = []

_ALL_LISTS = [win_ld, win_cast, xld, sqt_done, xn_done, xtr_done, xev_done,
              ip_mm, uev_done, conv_done, silu_done, ubf_done, uspill_done,
              rev_done, rspill_done, ub_ld, xp_mm, xp_ev, q_last_b, q_last_p,
              ur_ld, dtm_done, dl_done, w_done, wb_last, a_rdy, b_rdy,
              scan_done, p_rdy, acc_dma, y_mm, ut_free, sr_done, ybf_done, ysp_done,
              wo_ld, wo_cast, yld, xrld, op_mm, x2pre_done, sq5_done, x2_done,
              x2sp_done, x2tr_done, x2ev_done, rt_mm, rt_ev, cmb_done, w7_cast,
              fc_mm, gel_done, pr_mm, pr_ev, acc_ld, acc_rdy, o_done, o_store]
_ALL_DICTS = [fc_at, gel_at, pr_at, prev_ev, psf_last, psp_last]


def _reset_state():
    for obj in _ALL_LISTS:
        obj.clear()
    for d in _ALL_DICTS:
        d.clear()


def make_inputs(L0_attn_x, L0_attn_norm_w, L0_attn_in_proj_w, L0_attn_conv_w,
                L0_attn_conv_b, L0_attn_x_proj_w, L0_attn_dt_proj_w,
                L0_attn_dt_proj_b, L0_attn_A_log, L0_attn_D, L0_attn_out_proj_w,
                L0_ffn_centroids, L0_ffn_routing_bias, L0_ffn_Wfc_r,
                L0_ffn_Wproj_r, L0_ffn_Wfc_sh, L0_ffn_Wproj_sh):
    f32 = np.float32
    x = np.ascontiguousarray(L0_attn_x, dtype=f32)
    shared = {
        "w_in": np.ascontiguousarray(np.asarray(L0_attn_in_proj_w, f32).reshape(KD, 128, 2 * DI)),
        "normw": np.ascontiguousarray(np.asarray(L0_attn_norm_w, f32).reshape(KD, 128, 1)),
        "wcv": np.ascontiguousarray(np.asarray(L0_attn_conv_w, f32).reshape(NI, 128, 4)),
        "bcv": np.ascontiguousarray(np.asarray(L0_attn_conv_b, f32).reshape(NI, 128, 1)),
        "wxp": np.ascontiguousarray(np.asarray(L0_attn_x_proj_w, f32).reshape(NI, 128, 96)),
        "wdt": np.ascontiguousarray(np.asarray(L0_attn_dt_proj_w, f32)),
        "bdt": np.ascontiguousarray(np.asarray(L0_attn_dt_proj_b, f32).reshape(NI, 128, 1)),
        "alog": np.ascontiguousarray(np.asarray(L0_attn_A_log, f32).reshape(NI, 128, S)),
        "dvec": np.ascontiguousarray(np.asarray(L0_attn_D, f32).reshape(NI, 128, 1)),
        "wout": np.ascontiguousarray(np.asarray(L0_attn_out_proj_w, f32).reshape(NI, 128, DM)),
        "centT": np.ascontiguousarray(np.asarray(L0_ffn_centroids, f32).T.reshape(KD, 128, 8)),
        "biasr": np.ascontiguousarray(np.asarray(L0_ffn_routing_bias, f32).reshape(1, 8)),
        "wfcT": np.ascontiguousarray(
            np.concatenate([np.asarray(L0_ffn_Wfc_sh, f32), np.asarray(L0_ffn_Wfc_r, f32)], 0)
            .transpose(0, 2, 1)),
        "wprT": np.ascontiguousarray(
            np.concatenate([np.asarray(L0_ffn_Wproj_sh, f32), np.asarray(L0_ffn_Wproj_r, f32)], 0)
            .transpose(0, 2, 1)),
        "ident": np.eye(128, dtype=ml_dtypes.bfloat16),
    }
    return x, shared


def kernel(**inputs):
    x, shared = make_inputs(**inputs)
    _reset_state()
    nc = build_nc()
    in_maps = [dict(shared, x=np.ascontiguousarray(x[b])) for b in range(8)]
    res = run_bass_kernel_spmd(nc, in_maps, core_ids=list(range(8)))
    return np.stack([np.asarray(res.results[b]["out"]) for b in range(8)], axis=0)
